# version 01
# baseline (speedup 1.0000x reference)
import math
from collections import defaultdict

import numpy as np
import ml_dtypes
import concourse.bass as bass
import concourse.mybir as mybir
from concourse.bass_utils import run_bass_kernel_spmd

F32 = mybir.dt.float32
BF16 = mybir.dt.bfloat16
AF = mybir.ActivationFunctionType
ALU = mybir.AluOpType
AX = mybir.AxisListType

B = 2
S = 16384
D = 2048
TQ = 4096
TT = 1024
NPASS = TQ // TT
HSH = 8192
HRT = 1024
NE = 8
EPS = 1e-5

ENGINES = ("tensor", "vector", "scalar", "gpsimd", "sync")

DEBUG_PHASE = 0


class Prog:

    def __init__(self, nc):
        self.nc = nc
        self.ops = []
        self.cnt = defaultdict(int)
        self.sems = {}

    def sem(self, name):
        if name not in self.sems:
            self.sems[name] = self.nc.alloc_semaphore(name=name)
        return self.sems[name]

    def val(self, name):
        return self.cnt[name]

    def op(self, engine, make, waits=(), inc=None):
        waits = [(s, v) for (s, v) in waits if v > 0]
        for s, _ in waits:
            self.sem(s)
        ret = None
        if inc is not None:
            self.sem(inc[0])
            self.cnt[inc[0]] += inc[1]
            ret = self.cnt[inc[0]]
        sems = self.sems
        waits = tuple(waits)

        def fn(eng):
            for s, v in waits:
                eng.wait_ge(sems[s], v)
            ins = make(eng)
            if inc is not None:
                ins.then_inc(sems[inc[0]], inc[1])
        self.ops.append((engine, fn))
        return ret

    def wait(self, engine, waits):
        waits = [(s, v) for (s, v) in waits if v > 0]
        for s, _ in waits:
            self.sem(s)
        sems = self.sems
        waits = tuple(waits)

        def fn(eng):
            for s, v in waits:
                eng.wait_ge(sems[s], v)
        self.ops.append((engine, fn))

    def barrier(self):
        pre = {"gpsimd": [(s, self.cnt[s]) for s in ("xl", "wl") if s in self.sems],
               "sync": [(s, self.cnt[s]) for s in ("al", "st") if s in self.sems]}
        self.sem("bar")
        for e in ENGINES:
            self.op(e, lambda eng: eng.nop(), waits=pre.get(e, []), inc=("bar", 1))
        tgt = self.cnt["bar"]
        for e in ENGINES:
            self.wait(e, [("bar", tgt)])

    def replay(self, engine, eng):
        for e, fn in self.ops:
            if e == engine:
                fn(eng)


class Ctx:
    pass


def declare(nc, debug_phase):
    c = Ctx()
    dt = nc.dram_tensor
    c.xT = dt("xT", [D, S], F32, kind="ExternalInput")
    c.wqT = dt("wqT", [D, HPC * HD], F32, kind="ExternalInput")
    c.wkT = dt("wkT", [D, HPC * HD], F32, kind="ExternalInput")
    c.wvT = dt("wvT", [D, HPC * HD], F32, kind="ExternalInput")
    c.bq = dt("bq", [128, HPC], F32, kind="ExternalInput")
    c.bk = dt("bk", [128, HPC], F32, kind="ExternalInput")
    c.bv = dt("bv", [128, HPC], F32, kind="ExternalInput")
    c.woT = dt("woT", [D, D], F32, kind="ExternalInput")
    c.bo = dt("bo", [128, D // 128], F32, kind="ExternalInput")
    c.sw1T = dt("sw1T", [D, HSH], F32, kind="ExternalInput")
    c.sw2T = dt("sw2T", [D, HSH], F32, kind="ExternalInput")
    c.sw3T = dt("sw3T", [HSH, D], F32, kind="ExternalInput")
    c.w12T = dt("w12T", [NE, D, 2 * HRT], F32, kind="ExternalInput")
    c.w3T = dt("w3T", [NE, HRT, D], F32, kind="ExternalInput")
    c.rwT = dt("rwT", [D, NE], F32, kind="ExternalInput")
    c.ebias = dt("ebias", [128, NE], F32, kind="ExternalInput")
    c.ones_in = dt("ones", [128, 128], F32, kind="ExternalInput")
    c.identf = dt("identf", [128, 128], F32, kind="ExternalInput")
    c.yT = dt("yT", [D, TQ], F32, kind="ExternalOutput")

    c.xTq = dt("xTq", [D, TQ], F32, kind="ExternalInput")
    c.onesb_in = dt("onesb", [128, 128], BF16, kind="ExternalInput")
    c.qTd = dt("qTd", [HPC * HD, S], BF16)
    c.kTd = dt("kTd", [HPC * HD, S], BF16)
    c.vd = dt("vd", [S, HPC * HD], BF16)
    c.ctxl = dt("ctxl", [HPC * HD, S], BF16)
    c.ctxg = dt("ctxg", [D, S], BF16)
    c.qoff = dt("qoff", [1, 1], mybir.dt.int32, kind="ExternalInput")

    c.dbg = {}


class Ctx:
    pass


def declare(nc, debug_phase):
    c = Ctx()
    dt = nc.dram_tensor
    c.xTq = dt("xTq", [D, TQ], F32, kind="ExternalInput")
    c.wvT = dt("wvT", [D, D], F32, kind="ExternalInput")
    c.bv = dt("bv", [128, D // 128], F32, kind="ExternalInput")
    c.woT = dt("woT", [D, D], F32, kind="ExternalInput")
    c.bo = dt("bo", [128, D // 128], F32, kind="ExternalInput")
    c.sw1T = dt("sw1T", [D, HSH], F32, kind="ExternalInput")
    c.sw2T = dt("sw2T", [D, HSH], F32, kind="ExternalInput")
    c.sw3T = dt("sw3T", [HSH, D], F32, kind="ExternalInput")
    c.w12T = dt("w12T", [NE, D, 2 * HRT], F32, kind="ExternalInput")
    c.w3T = dt("w3T", [NE, HRT, D], F32, kind="ExternalInput")
    c.rwT = dt("rwT", [D, NE], F32, kind="ExternalInput")
    c.ebias = dt("ebias", [128, NE], F32, kind="ExternalInput")
    c.ones_in = dt("ones", [128, 128], F32, kind="ExternalInput")
    c.onesb_in = dt("onesb", [128, 128], BF16, kind="ExternalInput")
    c.identf = dt("identf", [128, 128], F32, kind="ExternalInput")
    c.idb_in = dt("idb", [128, 128], BF16, kind="ExternalInput")
    c.yT = dt("yT", [D, TQ], F32, kind="ExternalOutput")

    c.dbg = {}
    if debug_phase == 3:
        c.dbg["dr1"] = dt("dr1", [128, 16 * TT], BF16, kind="ExternalOutput")
        c.dbg["dcomb"] = dt("dcomb", [128, (TT // 128) * NE], F32, kind="ExternalOutput")
        c.dbg["dsh"] = dt("dsh", [128, 16 * TT], BF16, kind="ExternalOutput")
        c.dbg["dracc"] = dt("dracc", [128, 16 * TT], F32, kind="ExternalOutput")

    A = nc.alloc_sbuf_tensor
    c.ones_sb = A("ones_sb", [128, 128], F32)
    c.onesb_sb = A("onesb_sb", [128, 128], BF16)
    c.idf_sb = A("idf_sb", [128, 128], F32)
    c.idb_sb = A("idb_sb", [128, 128], BF16)
    c.ebias_sb = A("ebias_sb", [128, NE], F32)
    c.bv_sb = A("bv_sb", [128, D // 128], F32)
    c.bo_sb = A("bo_sb", [128, D // 128], F32)
    c.rw_sb = A("rw_sb", [128, (D // 128) * NE], BF16)
    c.wvs_sb = A("wvs_sb", [128, 16 * 128], BF16)
    c.wos_sb = A("wos_sb", [128, D], BF16)
    c.bvs_sb = A("bvs_sb", [128, 1], F32)
    c.vsum_sb = A("vsum_sb", [128, 512], BF16)
    c.t_sb = A("t_sb", [128, 512], BF16)
    c.u_sb = A("u_sb", [128, 512], BF16)
    c.mrow = A("mrow", [128, 512], F32)
    c.abrow = A("abrow", [128, 1024], F32)
    c.lnrow = A("lnrow", [128, 1024], F32)
    c.lnrow2 = A("lnrow2", [128, 1024], F32)
    c.comb_sb = A("comb_sb", [128, (TT // 128) * NE], F32)
    c.tmp8 = [A(f"tmp8_{i}", [128, NE], F32) for i in range(3)]
    c.cols4 = A("cols4", [128, 4], F32)

    ABASE = 49152

    def AT(name, shape, dtyp, off):
        return nc.alloc_sbuf_tensor_at(name, shape, dtyp, offset=ABASE + off)

    c.r1_sb = AT("r1_sb", [128, 16 * TT], BF16, 0)
    c.sh_sb = AT("sh_sb", [128, 16 * TT], BF16, 32768)
    c.hs_sb = AT("hs_sb", [128, 32 * TT], BF16, 65536)
    c.racc_sb = AT("racc_sb", [128, 16 * TT], F32, 65536)
    c.he_sb = AT("he_sb", [128, 8 * TT], BF16, 131072)
    c.cbb_sb = AT("cbb_sb", [128, TT], F32, 147456)
    c.w12c = [AT(f"w12c{i}", [128, 2048], BF16, 151552 + 4096 * i) for i in range(4)]
    c.w3c = [AT("w3c0", [128, 1024], BF16, 167936), AT("w3c1", [128, 1024], BF16, 169984)]
    c.xbf_sb = AT("xbf_sb", [128, 8192], BF16, 32768)
    c.xres_sb = AT("xres_sb", [128, 8192], F32, 49152)
    c.rpre_sb = AT("rpre_sb", [128, 8192], F32, 81920)
    c.sq_sb = AT("sq_sb", [128, 8192], BF16, 114688)
    c.fold_sb = AT("fold_sb", [128, 2048], F32, 131072)
    c.fold2_sb = AT("fold2_sb", [128, 128], F32, 139264)
    c.wofold_sb = AT("wofold_sb", [128, 16 * 512], BF16, 139776)
    c.sw1c = [AT("sw1c0", [128, 2048], BF16, 131072), AT("sw1c1", [128, 2048], BF16, 135168)]
    c.sw2c = [AT("sw2c0", [128, 2048], BF16, 139264), AT("sw2c1", [128, 2048], BF16, 143360)]
    c.sw3c = [AT("sw3c0", [128, 4096], BF16, 147456), AT("sw3c1", [128, 4096], BF16, 155648)]

    c.ps = nc.alloc_psum_tensor("ps", [128, 4096], F32)
    return c


def bank(c, i, width=512, off=0):
    return c.ps[:, i * 512 + off: i * 512 + off + width]


def build_consts(p, c):
    for sb, dram in ((c.ones_sb, c.ones_in), (c.onesb_sb, c.onesb_in),
                     (c.idf_sb, c.identf), (c.idb_sb, c.idb_in),
                     (c.ebias_sb, c.ebias), (c.bv_sb, c.bv), (c.bo_sb, c.bo)):
        p.op("sync", lambda e, sb=sb, dram=dram: e.dma_start(out=sb[:, :], in_=dram.ap()),
             inc=("al", 16))
    p.op("gpsimd", lambda e: e.dma_start(
        out=c.rw_sb[:, :].rearrange("p (c e) -> p c e", c=D // 128),
        in_=c.rwT.ap().rearrange("(c p) e -> p c e", p=128)), inc=("wl", 16))


def build_fold(p, c):
    al0 = p.val("al")
    p.op("vector", lambda e: e.reduce_sum(c.bvs_sb[:, 0:1], c.bv_sb[:, :], AX.X),
         waits=[("al", al0)], inc=("ve", 1))
    for dch in range(16):
        wl_v = p.op("gpsimd", lambda e, dch=dch: e.dma_start(
            out=c.fold_sb[:, :], in_=c.wvT.ap()[dch * 128:(dch + 1) * 128, :]),
            waits=[("ve", p.val("ve"))] if dch > 0 else [], inc=("wl", 16))
        p.op("vector", lambda e: e.reduce_sum(
            c.fold2_sb[:, :], c.fold_sb[:, :].rearrange("p (i o) -> p o i", i=16), AX.X),
            waits=[("wl", wl_v), ("ve", p.val("ve"))], inc=("ve", 1))
        p.op("vector", lambda e, dch=dch: e.tensor_copy(
            c.wvs_sb[:, dch * 128:(dch + 1) * 128], c.fold2_sb[:, :]),
            waits=[("ve", p.val("ve"))], inc=("ve", 1))
    for ob in range(4):
        wl_v = p.op("gpsimd", lambda e, ob=ob: e.dma_start(
            out=c.wofold_sb[:, :].rearrange("p (i o) -> p i o", i=16),
            in_=c.wvT.ap() if False else c.woT.ap().rearrange(
                "(i p) o -> p i o", p=128)[:, :, ob * 512:(ob + 1) * 512]),
            waits=[("ex", p.val("ex"))], inc=("wl", 16))
        for i in range(16):
            mm_v = p.op("tensor", lambda e, i=i, ob=ob: e.matmul(
                bank(c, ob % 2), c.idb_sb[:, :],
                c.wofold_sb[:, i * 512:(i + 1) * 512],
                start=(i == 0), stop=(i == 15)),
                waits=[("wl", wl_v), ("ex", p.val("ex"))] if i == 0 else (),
                inc=("pp", 1) if i == 15 else None)
        p.op("scalar", lambda e, ob=ob: e.activation(
            c.wos_sb[:, ob * 512:(ob + 1) * 512], bank(c, ob % 2), AF.Identity),
            waits=[("pp", mm_v)], inc=("ex", 1))
    p.barrier()


def build_ffn(p, c):
    def r1s(ch, tc):
        return c.r1_sb[:, ch * TT + tc * 512: ch * TT + tc * 512 + 512]

    bank6_gate = []
    for pi in range(NPASS):
        u0 = pi * TT
        bank_gate = [[] for _ in range(8)]
        apply_done = 0
        for tc in range(2):
            xl_v = p.op("gpsimd", lambda e, u0=u0, tc=tc: e.dma_start(
                out=c.xbf_sb[:, :].rearrange("p (ch t) -> p ch t", ch=16),
                in_=c.xTq.ap().rearrange("(ch p) t -> p ch t", p=128)[
                    :, :, u0 + tc * 512: u0 + tc * 512 + 512]),
                waits=[("ve", apply_done)], inc=("wl", 16))
            xr_v = p.op("sync", lambda e, u0=u0, tc=tc: e.dma_start(
                out=c.xres_sb[:, :].rearrange("p (ch t) -> p ch t", ch=16),
                in_=c.xTq.ap().rearrange("(ch p) t -> p ch t", p=128)[
                    :, :, u0 + tc * 512: u0 + tc * 512 + 512]),
                waits=[("ve", apply_done)], inc=("al", 16))
            for ch in range(16):
                w = ([("wl", xl_v)] + bank6_gate) if ch == 0 else ()
                vs_v = p.op("tensor", lambda e, ch=ch: e.matmul(
                    bank(c, 6), c.wvs_sb[:, ch * 128:(ch + 1) * 128],
                    c.xbf_sb[:, ch * 512:(ch + 1) * 512],
                    start=(ch == 0), stop=(ch == 15)),
                    waits=w, inc=("pp", 1) if ch == 15 else None)
            vs_e = p.op("scalar", lambda e: e.activation(
                c.vsum_sb[:, :], bank(c, 6), AF.Identity, bias=c.bvs_sb[:, 0:1]),
                waits=[("pp", vs_v)], inc=("ex", 1))
            bank6_gate = [("ex", vs_e)]
            ev_of = {}
            ve_of = {}
            sq_of = {}
            sumx_v = sumq_v = None
            for oc in range(16):
                bnk = oc % 4
                pp_v = p.op("tensor", lambda e, oc=oc, bnk=bnk: e.matmul(
                    bank(c, bnk), c.wos_sb[:, oc * 128:(oc + 1) * 128],
                    c.vsum_sb[:, :], start=True, stop=True),
                    waits=([("ex", vs_e)] + bank_gate[bnk]), inc=("pp", 1))
                ev_of[oc] = p.op("scalar", lambda e, bnk=bnk, oc=oc: e.activation(
                    c.rpre_sb[:, oc * 512:(oc + 1) * 512], bank(c, bnk), AF.Identity,
                    bias=c.bo_sb[:, oc:oc + 1], scale=1.0),
                    waits=[("pp", pp_v), ("ve", apply_done), ("al", xr_v)], inc=("ex", 1))
                ve_of[oc] = p.op("vector", lambda e, oc=oc: e.tensor_tensor(
                    c.rpre_sb[:, oc * 512:(oc + 1) * 512],
                    c.xres_sb[:, oc * 512:(oc + 1) * 512],
                    c.rpre_sb[:, oc * 512:(oc + 1) * 512], ALU.add),
                    waits=[("ex", ev_of[oc])], inc=("ve", 1))
                bank_gate[bnk] = [("ex", ev_of[oc])]
                sq_of[oc] = p.op("scalar", lambda e, oc=oc: e.activation(
                    c.sq_sb[:, oc * 512:(oc + 1) * 512],
                    c.rpre_sb[:, oc * 512:(oc + 1) * 512], AF.Square),
                    waits=[("ve", ve_of[oc])], inc=("ex", 1))
                sumx_v = p.op("tensor", lambda e, oc=oc: e.matmul(
                    c.ps[0:1, 4 * 512:4 * 512 + 512], c.ones_sb[:, 0:1],
                    c.rpre_sb[:, oc * 512:(oc + 1) * 512],
                    start=(oc == 0), stop=(oc == 15)),
                    waits=[("ve", ve_of[oc])], inc=("pp", 1) if oc == 15 else None)
                sumq_v = p.op("tensor", lambda e, oc=oc: e.matmul(
                    c.ps[0:1, 5 * 512:5 * 512 + 512], c.onesb_sb[:, 0:1],
                    c.sq_sb[:, oc * 512:(oc + 1) * 512],
                    start=(oc == 0), stop=(oc == 15)),
                    waits=[("ex", sq_of[oc])], inc=("pp", 1) if oc == 15 else None)
            sums_done = p.val("pp")
            m1 = p.op("scalar", lambda e: e.activation(
                c.lnrow[0:1, 0:512], c.ps[0:1, 4 * 512:4 * 512 + 512], AF.Identity,
                scale=1.0 / D), waits=[("pp", sums_done)], inc=("ex", 1))
            m2 = p.op("scalar", lambda e: e.activation(
                c.lnrow[0:1, 512:1024], c.ps[0:1, 5 * 512:5 * 512 + 512], AF.Identity,
                scale=1.0 / D), inc=("ex", 1))
            v1 = p.op("vector", lambda e: e.tensor_mul(
                c.lnrow2[0:1, 0:512], c.lnrow[0:1, 0:512], c.lnrow[0:1, 0:512]),
                waits=[("ex", m2)], inc=("ve", 1))
            v2 = p.op("vector", lambda e: e.tensor_sub(
                c.lnrow2[0:1, 0:512], c.lnrow[0:1, 512:1024], c.lnrow2[0:1, 0:512]),
                waits=[("ve", v1)], inc=("ve", 1))
            v2b = p.op("vector", lambda e: e.tensor_single_scalar(
                c.lnrow2[0:1, 0:512], c.lnrow2[0:1, 0:512], EPS, ALU.add),
                waits=[("ve", v2)], inc=("ve", 1))
            s3 = p.op("scalar", lambda e: e.activation(
                c.lnrow2[0:1, 512:1024], c.lnrow2[0:1, 0:512], AF.Sqrt),
                waits=[("ve", v2b)], inc=("ex", 1))
            v4 = p.op("vector", lambda e: e.reciprocal(
                c.lnrow2[0:1, 0:512], c.lnrow2[0:1, 512:1024]),
                waits=[("ex", s3)], inc=("ve", 1))
            v5 = p.op("vector", lambda e: e.scalar_tensor_tensor(
                c.lnrow2[0:1, 512:1024], c.lnrow[0:1, 0:512], -1.0, c.lnrow2[0:1, 0:512],
                ALU.mult, ALU.mult), waits=[("ve", v4)], inc=("ve", 1))
            bA = p.op("tensor", lambda e: e.matmul(
                c.ps[:, 4 * 512:4 * 512 + 512], c.ones_sb[0:1, 0:128],
                c.lnrow2[0:1, 0:512], start=True, stop=True),
                waits=[("ve", v5)], inc=("pp", 1))
            bB = p.op("tensor", lambda e: e.matmul(
                c.ps[:, 5 * 512:5 * 512 + 512], c.ones_sb[0:1, 0:128],
                c.lnrow2[0:1, 512:1024], start=True, stop=True), inc=("pp", 1))
            eA = p.op("scalar", lambda e: e.activation(
                c.abrow[:, 0:512], c.ps[:, 4 * 512:4 * 512 + 512], AF.Identity),
                waits=[("pp", bB)], inc=("ex", 1))
            eB = p.op("scalar", lambda e: e.activation(
                c.abrow[:, 512:1024], c.ps[:, 5 * 512:5 * 512 + 512], AF.Identity),
                inc=("ex", 1))
            for ch in range(16):
                p.op("vector", lambda e, ch=ch, tc=tc: e.tensor_mask(
                    None), waits=()) if False else None
                p.op("vector", lambda e, ch=ch, tc=tc: e.scalar_tensor_tensor(
                    r1s(ch, tc), c.rpre_sb[:, ch * 512:(ch + 1) * 512], 1.0,
                    c.abrow[:, 0:512], ALU.mult, ALU.mult),
                    waits=[("ex", eB)] if ch == 0 else ())
                apply_done = p.op("vector", lambda e, ch=ch, tc=tc: e.tensor_tensor(
                    r1s(ch, tc), c.abrow[:, 512:1024], r1s(ch, tc), ALU.add),
                    inc=("ve", 1))
            for ch in range(16):
                p.op("tensor", lambda e, ch=ch, tc=tc: e.matmul(
                    c.ps[0:8, 6 * 512:6 * 512 + 512], c.rw_sb[:, ch * 8:(ch + 1) * 8],
                    r1s(ch, tc), start=(ch == 0), stop=(ch == 15)),
                    waits=[("ve", apply_done)] if ch == 0 else (),
                    inc=("pp", 1) if ch == 15 else None)
            rt_v = p.val("pp")
            sg = p.op("scalar", lambda e: e.activation(
                c.mrow[0:8, 0:512], c.ps[0:8, 6 * 512:6 * 512 + 512], AF.Sigmoid),
                waits=[("pp", rt_v)], inc=("ex", 1))
            for s4 in range(4):
                p.op("tensor", lambda e, s4=s4: e.transpose(
                    c.ps[:, 7 * 512 + s4 * 8: 7 * 512 + (s4 + 1) * 8],
                    c.mrow[0:8, s4 * 128:(s4 + 1) * 128], c.idf_sb[0:8, 0:8]),
                    waits=[("ex", sg)] if s4 == 0 else (),
                    inc=("pp", 1) if s4 == 3 else None)
            tr_v = p.val("pp")
            for s4 in range(4):
                sub = tc * 4 + s4

                def vstep(mk, extra=()):
                    w = [("ve", p.val("ve"))] + list(extra)
                    p.op("vector", mk, waits=w, inc=("ve", 1))
                vstep(lambda e, s4=s4: e.tensor_copy(
                    c.tmp8[0][:, :], c.ps[:, 7 * 512 + s4 * 8: 7 * 512 + (s4 + 1) * 8]),
                    extra=[("pp", tr_v)] if s4 == 0 else ())
                vstep(lambda e: e.tensor_add(
                    c.tmp8[1][:, :], c.tmp8[0][:, :], c.ebias_sb[:, :]))
                vstep(lambda e: e.reduce_max(
                    c.cols4[:, 0:1], c.tmp8[1][:, :], AX.X))
                vstep(lambda e: e.tensor_single_scalar(
                    c.tmp8[2][:, :], c.tmp8[1][:, :], c.cols4[:, 0:1], ALU.is_lt))
                vstep(lambda e: e.tensor_mul(
                    c.tmp8[1][:, :], c.tmp8[1][:, :], c.tmp8[2][:, :]))
                vstep(lambda e: e.tensor_scalar(
                    c.tmp8[2][:, :], c.tmp8[2][:, :], -1.0, 1e30, ALU.add, ALU.mult))
                vstep(lambda e: e.tensor_add(
                    c.tmp8[1][:, :], c.tmp8[2][:, :], c.tmp8[1][:, :]))
                vstep(lambda e: e.reduce_max(
                    c.cols4[:, 1:2], c.tmp8[1][:, :], AX.X))
                vstep(lambda e: e.tensor_add(
                    c.tmp8[1][:, :], c.tmp8[0][:, :], c.ebias_sb[:, :]))
                vstep(lambda e: e.tensor_single_scalar(
                    c.tmp8[2][:, :], c.tmp8[1][:, :], c.cols4[:, 1:2], ALU.is_ge))
                vstep(lambda e: e.tensor_mul(
                    c.tmp8[2][:, :], c.tmp8[0][:, :], c.tmp8[2][:, :]))
                vstep(lambda e: e.reduce_sum(
                    c.cols4[:, 2:3], c.tmp8[2][:, :], AX.X))
                vstep(lambda e: e.tensor_single_scalar(
                    c.cols4[:, 2:3], c.cols4[:, 2:3], 1e-9, ALU.add))
                vstep(lambda e: e.reciprocal(
                    c.cols4[:, 3:4], c.cols4[:, 2:3]))
                vstep(lambda e, sub=sub: e.tensor_scalar_mul(
                    c.comb_sb[:, sub * 8:(sub + 1) * 8], c.tmp8[2][:, :], c.cols4[:, 3:4]))
            rtr_done = p.val("ve")
        p.barrier()
        if c.dbg and "dr1" in c.dbg and pi == 0:
            p.op("sync", lambda e: e.dma_start(out=c.dbg["dr1"].ap(), in_=c.r1_sb[:, :]), inc=("st", 16))
            p.op("sync", lambda e: e.dma_start(out=c.dbg["dcomb"].ap(), in_=c.comb_sb[:, :]), inc=("st", 16))
            p.wait("sync", [("st", p.val("st"))])
            p.barrier()

        swc_gate = [[], []]
        down_a_done = None
        last_mul = 0
        sh_of = {}
        for Hf in range(2):
            for hc in range(32):
                gc = Hf * 32 + hc
                buf = hc % 2
                p.op("gpsimd", lambda e, gc=gc, buf=buf: e.dma_start(
                    out=c.sw1c[buf][:, :].rearrange("p (ch o) -> p ch o", ch=16),
                    in_=c.sw1T.ap().rearrange("(ch p) o -> p ch o", p=128)[:, :, gc * 128:(gc + 1) * 128]),
                    waits=swc_gate[buf], inc=("wl", 16))
                wlv = p.op("gpsimd", lambda e, gc=gc, buf=buf: e.dma_start(
                    out=c.sw2c[buf][:, :].rearrange("p (ch o) -> p ch o", ch=16),
                    in_=c.sw2T.ap().rearrange("(ch p) o -> p ch o", p=128)[:, :, gc * 128:(gc + 1) * 128]),
                    inc=("wl", 16))
                hw = [("wl", wlv)]
                if Hf == 1 and hc == 0:
                    hw.append(("pp", down_a_done))
                for tc in range(2):
                    gbnk, ubnk = 2 * tc, 2 * tc + 1
                    for ch in range(16):
                        w = (hw + bank_gate[gbnk]) if ch == 0 else ()
                        gv = p.op("tensor", lambda e, ch=ch, buf=buf, gbnk=gbnk, tc=tc: e.matmul(
                            bank(c, gbnk), c.sw1c[buf][:, ch * 128:(ch + 1) * 128],
                            r1s(ch, tc), start=(ch == 0), stop=(ch == 15)),
                            waits=w, inc=("pp", 1) if ch == 15 else None)
                    for ch in range(16):
                        w = bank_gate[ubnk] if ch == 0 else ()
                        uv = p.op("tensor", lambda e, ch=ch, buf=buf, ubnk=ubnk, tc=tc: e.matmul(
                            bank(c, ubnk), c.sw2c[buf][:, ch * 128:(ch + 1) * 128],
                            r1s(ch, tc), start=(ch == 0), stop=(ch == 15)),
                            waits=w, inc=("pp", 1) if ch == 15 else None)
                    sil = p.op("scalar", lambda e, gbnk=gbnk: e.activation(
                        c.t_sb[:, :], bank(c, gbnk), AF.Silu),
                        waits=[("pp", gv), ("ve", last_mul)], inc=("ex", 1))
                    mul = p.op("vector", lambda e, ubnk=ubnk, hc=hc, tc=tc: e.tensor_tensor(
                        c.hs_sb[:, hc * 1024 + tc * 512: hc * 1024 + tc * 512 + 512],
                        c.t_sb[:, :], bank(c, ubnk), ALU.mult),
                        waits=[("pp", uv), ("ex", sil)], inc=("ve", 1))
                    last_mul = mul
                    bank_gate[gbnk] = [("ex", sil)]
                    bank_gate[ubnk] = [("ve", mul)]
                swc_gate[buf] = [("pp", p.val("pp"))]
            hs_done = p.val("ve")
            sw3_gate = [[], []]
            for dt in range(16):
                buf = dt % 2
                w3v = p.op("gpsimd", lambda e, dt=dt, Hf=Hf, buf=buf: e.dma_start(
                    out=c.sw3c[buf][:, :].rearrange("p (ch o) -> p ch o", ch=32),
                    in_=c.sw3T.ap().rearrange("(ch p) o -> p ch o", p=128)[
                        :, Hf * 32:(Hf + 1) * 32, dt * 128:(dt + 1) * 128]),
                    waits=sw3_gate[buf], inc=("wl", 16))
                for tc in range(2):
                    bnk = 4 + 2 * (dt % 2) + tc
                    for hch in range(32):
                        w = ([("wl", w3v), ("ve", hs_done)] + bank_gate[bnk]) if hch == 0 else ()
                        dv = p.op("tensor", lambda e, hch=hch, buf=buf, bnk=bnk, tc=tc: e.matmul(
                            bank(c, bnk), c.sw3c[buf][:, hch * 128:(hch + 1) * 128],
                            c.hs_sb[:, hch * 1024 + tc * 512: hch * 1024 + tc * 512 + 512],
                            start=(hch == 0), stop=(hch == 31)),
                            waits=w, inc=("pp", 1) if hch == 31 else None)
                    if Hf == 0:
                        sh_of[(dt, tc)] = p.op("scalar", lambda e, dt=dt, tc=tc, bnk=bnk: e.copy(
                            c.sh_sb[:, dt * 1024 + tc * 512: dt * 1024 + tc * 512 + 512],
                            bank(c, bnk)), waits=[("pp", dv)], inc=("ex", 1))
                        bank_gate[bnk] = [("ex", sh_of[(dt, tc)])]
                    else:
                        shv = p.op("vector", lambda e, dt=dt, tc=tc, bnk=bnk: e.tensor_tensor(
                            c.sh_sb[:, dt * 1024 + tc * 512: dt * 1024 + tc * 512 + 512],
                            bank(c, bnk),
                            c.sh_sb[:, dt * 1024 + tc * 512: dt * 1024 + tc * 512 + 512],
                            ALU.add),
                            waits=[("pp", dv), ("ex", sh_of[(dt, tc)])], inc=("ve", 1))
                        bank_gate[bnk] = [("ve", shv)]
                sw3_gate[buf] = [("pp", p.val("pp"))]
            if Hf == 0:
                down_a_done = p.val("pp")
        p.barrier()
        if c.dbg and "dsh" in c.dbg and pi == 0:
            p.op("sync", lambda e: e.dma_start(out=c.dbg["dsh"].ap(), in_=c.sh_sb[:, :]), inc=("st", 16))
            p.wait("sync", [("st", p.val("st"))])
            p.barrier()

        racc_gate = {}
        for e_i in range(NE):
            for sub in range(8):
                p.op("tensor", lambda e, sub=sub, e_i=e_i: e.transpose(
                    c.ps[0:1, 6 * 512 + sub * 128: 6 * 512 + (sub + 1) * 128],
                    c.comb_sb[:, sub * 8 + e_i: sub * 8 + e_i + 1], c.idf_sb[:, :]),
                    waits=([("ex", p.val("ex")), ("ve", p.val("ve"))] if sub == 0 else ()),
                    inc=("pp", 1) if sub == 7 else None)
            trv = p.val("pp")
            cpv = p.op("vector", lambda e: e.tensor_copy(
                c.cbb_sb[0:1, 0:1024],
                c.ps[0:1, 6 * 512: 6 * 512 + 1024]), waits=[("pp", trv)], inc=("ve", 1))
            b1 = p.op("tensor", lambda e: e.matmul(
                c.ps[:, 6 * 512:6 * 512 + 512], c.ones_sb[0:1, 0:128],
                c.cbb_sb[0:1, 0:512], start=True, stop=True),
                waits=[("ve", cpv)], inc=("pp", 1))
            b2 = p.op("tensor", lambda e: e.matmul(
                c.ps[:, 7 * 512:7 * 512 + 512], c.ones_sb[0:1, 0:128],
                c.cbb_sb[0:1, 512:1024], start=True, stop=True), inc=("pp", 1))
            c1 = p.op("scalar", lambda e: e.activation(
                c.cbb_sb[:, 0:512], c.ps[:, 6 * 512:6 * 512 + 512], AF.Identity),
                waits=[("pp", b2)], inc=("ex", 1))
            c2 = p.op("scalar", lambda e: e.activation(
                c.cbb_sb[:, 512:1024], c.ps[:, 7 * 512:7 * 512 + 512], AF.Identity),
                inc=("ex", 1))
            cbb_ready = c2

            w12_gate = [[] for _ in range(4)]
            for j in range(8):
                bufa, bufb = (2 * j) % 4, (2 * j + 1) % 4
                wa = p.op("gpsimd", lambda e, j=j, e_i=e_i, bufa=bufa: e.dma_start(
                    out=c.w12c[bufa][:, :].rearrange("p (ch o) -> p ch o", ch=16),
                    in_=c.w12T.ap()[e_i].rearrange("(ch p) o -> p ch o", p=128)[:, :, j * 128:(j + 1) * 128]),
                    waits=w12_gate[bufa], inc=("wl", 16))
                wb = p.op("gpsimd", lambda e, j=j, e_i=e_i, bufb=bufb: e.dma_start(
                    out=c.w12c[bufb][:, :].rearrange("p (ch o) -> p ch o", ch=16),
                    in_=c.w12T.ap()[e_i].rearrange("(ch p) o -> p ch o", p=128)[:, :, (j + 8) * 128:(j + 9) * 128]),
                    waits=w12_gate[bufb], inc=("wl", 16))
                for tc in range(2):
                    b1k, b2k = 2 * tc, 2 * tc + 1
                    for ch in range(16):
                        w = ([("wl", wb)] + bank_gate[b1k]) if ch == 0 else ()
                        h1v = p.op("tensor", lambda e, ch=ch, bufa=bufa, b1k=b1k, tc=tc: e.matmul(
                            bank(c, b1k), c.w12c[bufa][:, ch * 128:(ch + 1) * 128],
                            r1s(ch, tc), start=(ch == 0), stop=(ch == 15)),
                            waits=w, inc=("pp", 1) if ch == 15 else None)
                    for ch in range(16):
                        w = bank_gate[b2k] if ch == 0 else ()
                        h2v = p.op("tensor", lambda e, ch=ch, bufb=bufb, b2k=b2k, tc=tc: e.matmul(
                            bank(c, b2k), c.w12c[bufb][:, ch * 128:(ch + 1) * 128],
                            r1s(ch, tc), start=(ch == 0), stop=(ch == 15)),
                            waits=w, inc=("pp", 1) if ch == 15 else None)
                    sil = p.op("scalar", lambda e, b1k=b1k: e.activation(
                        c.t_sb[:, :], bank(c, b1k), AF.Silu),
                        waits=[("pp", h1v), ("ex", cbb_ready), ("ve", last_mul)],
                        inc=("ex", 1))
                    mu1 = p.op("vector", lambda e, b2k=b2k: e.tensor_tensor(
                        c.u_sb[:, :], c.t_sb[:, :], bank(c, b2k), ALU.mult),
                        waits=[("pp", h2v), ("ex", sil)], inc=("ve", 1))
                    last_mul = mu1
                    mu2 = p.op("vector", lambda e, j=j, tc=tc: e.tensor_tensor(
                        c.he_sb[:, j * 1024 + tc * 512: j * 1024 + tc * 512 + 512],
                        c.u_sb[:, :], c.cbb_sb[:, tc * 512:(tc + 1) * 512], ALU.mult),
                        inc=("ve", 1))
                    bank_gate[b1k] = [("ex", sil)]
                    bank_gate[b2k] = [("ve", mu1)]
                w12_gate[bufa] = [("pp", p.val("pp"))]
                w12_gate[bufb] = [("pp", p.val("pp"))]
            he_all = p.val("ve")
            w3_gate = [[], []]
            for dt in range(16):
                buf = dt % 2
                w3v = p.op("gpsimd", lambda e, dt=dt, e_i=e_i, buf=buf: e.dma_start(
                    out=c.w3c[buf][:, :].rearrange("p (ch o) -> p ch o", ch=8),
                    in_=c.w3T.ap()[e_i].rearrange("(ch p) o -> p ch o", p=128)[:, :, dt * 128:(dt + 1) * 128]),
                    waits=w3_gate[buf], inc=("wl", 16))
                for tc in range(2):
                    bnk = 4 + 2 * (dt % 2) + tc
                    for hch in range(8):
                        w = ([("wl", w3v), ("ve", he_all)] + bank_gate[bnk]) if hch == 0 else ()
                        dv = p.op("tensor", lambda e, hch=hch, buf=buf, bnk=bnk, tc=tc: e.matmul(
                            bank(c, bnk), c.w3c[buf][:, hch * 128:(hch + 1) * 128],
                            c.he_sb[:, hch * 1024 + tc * 512: hch * 1024 + tc * 512 + 512],
                            start=(hch == 0), stop=(hch == 7)),
                            waits=w, inc=("pp", 1) if hch == 7 else None)
                    slot = c.racc_sb[:, dt * 1024 + tc * 512: dt * 1024 + tc * 512 + 512]
                    if e_i == 0:
                        rv = p.op("vector", lambda e, slot=slot, bnk=bnk, dt=dt, tc=tc: e.tensor_tensor(
                            slot, bank(c, bnk),
                            c.sh_sb[:, dt * 1024 + tc * 512: dt * 1024 + tc * 512 + 512],
                            ALU.add), waits=[("pp", dv)], inc=("ve", 1))
                    else:
                        rv = p.op("vector", lambda e, slot=slot, bnk=bnk: e.tensor_tensor(
                            slot, bank(c, bnk), slot, ALU.add),
                            waits=[("pp", dv)] + racc_gate.get((dt, tc), []), inc=("ve", 1))
                    racc_gate[(dt, tc)] = [("ve", rv)]
                    bank_gate[bnk] = [("ve", rv)]
                w3_gate[buf] = [("pp", p.val("pp"))]
        p.barrier()
        if c.dbg and "dracc" in c.dbg and pi == 0:
            p.op("sync", lambda e: e.dma_start(out=c.dbg["dracc"].ap(), in_=c.racc_sb[:, :]), inc=("st", 16))
            p.wait("sync", [("st", p.val("st"))])
            p.barrier()

        for tc in range(2):
            sumx_v = sumq_v = None
            rs_of = {}
            sq_of = {}
            for ch in range(16):
                slot = c.racc_sb[:, ch * 1024 + tc * 512: ch * 1024 + tc * 512 + 512]
                rs_of[ch] = p.op("vector", lambda e, slot=slot, ch=ch, tc=tc: e.tensor_tensor(
                    slot, r1s(ch, tc), slot, ALU.add), inc=("ve", 1))
                sq_of[ch] = p.op("scalar", lambda e, slot=slot, ch=ch: e.activation(
                    c.he_sb[:, ch * 512:(ch + 1) * 512], slot, AF.Square),
                    waits=[("ve", rs_of[ch])], inc=("ex", 1))
                sumx_v = p.op("tensor", lambda e, slot=slot, ch=ch: e.matmul(
                    c.ps[0:1, 4 * 512:4 * 512 + 512], c.ones_sb[:, 0:1], slot,
                    start=(ch == 0), stop=(ch == 15)),
                    waits=[("ve", rs_of[ch])], inc=("pp", 1) if ch == 15 else None)
                sumq_v = p.op("tensor", lambda e, ch=ch: e.matmul(
                    c.ps[0:1, 5 * 512:5 * 512 + 512], c.onesb_sb[:, 0:1],
                    c.he_sb[:, ch * 512:(ch + 1) * 512],
                    start=(ch == 0), stop=(ch == 15)),
                    waits=[("ex", sq_of[ch])], inc=("pp", 1) if ch == 15 else None)
            sums_done = p.val("pp")
            m1 = p.op("scalar", lambda e: e.activation(
                c.lnrow[0:1, 0:512], c.ps[0:1, 4 * 512:4 * 512 + 512], AF.Identity,
                scale=1.0 / D), waits=[("pp", sums_done)], inc=("ex", 1))
            m2 = p.op("scalar", lambda e: e.activation(
                c.lnrow[0:1, 512:1024], c.ps[0:1, 5 * 512:5 * 512 + 512], AF.Identity,
                scale=1.0 / D), inc=("ex", 1))
            v1 = p.op("vector", lambda e: e.tensor_mul(
                c.lnrow2[0:1, 0:512], c.lnrow[0:1, 0:512], c.lnrow[0:1, 0:512]),
                waits=[("ex", m2)], inc=("ve", 1))
            v2 = p.op("vector", lambda e: e.tensor_sub(
                c.lnrow2[0:1, 0:512], c.lnrow[0:1, 512:1024], c.lnrow2[0:1, 0:512]),
                waits=[("ve", v1)], inc=("ve", 1))
            v2b = p.op("vector", lambda e: e.tensor_single_scalar(
                c.lnrow2[0:1, 0:512], c.lnrow2[0:1, 0:512], EPS, ALU.add),
                waits=[("ve", v2)], inc=("ve", 1))
            s3 = p.op("scalar", lambda e: e.activation(
                c.lnrow2[0:1, 512:1024], c.lnrow2[0:1, 0:512], AF.Sqrt),
                waits=[("ve", v2b)], inc=("ex", 1))
            v4 = p.op("vector", lambda e: e.reciprocal(
                c.lnrow2[0:1, 0:512], c.lnrow2[0:1, 512:1024]),
                waits=[("ex", s3)], inc=("ve", 1))
            v5 = p.op("vector", lambda e: e.scalar_tensor_tensor(
                c.lnrow2[0:1, 512:1024], c.lnrow[0:1, 0:512], -1.0, c.lnrow2[0:1, 0:512],
                ALU.mult, ALU.mult), waits=[("ve", v4)], inc=("ve", 1))
            bA = p.op("tensor", lambda e: e.matmul(
                c.ps[:, 4 * 512:4 * 512 + 512], c.ones_sb[0:1, 0:128],
                c.lnrow2[0:1, 0:512], start=True, stop=True),
                waits=[("ve", v5)], inc=("pp", 1))
            bB = p.op("tensor", lambda e: e.matmul(
                c.ps[:, 5 * 512:5 * 512 + 512], c.ones_sb[0:1, 0:128],
                c.lnrow2[0:1, 512:1024], start=True, stop=True), inc=("pp", 1))
            eA = p.op("scalar", lambda e: e.activation(
                c.abrow[:, 0:512], c.ps[:, 4 * 512:4 * 512 + 512], AF.Identity),
                waits=[("pp", bB)], inc=("ex", 1))
            eB = p.op("scalar", lambda e: e.activation(
                c.abrow[:, 512:1024], c.ps[:, 5 * 512:5 * 512 + 512], AF.Identity),
                inc=("ex", 1))
            fin = None
            for ch in range(16):
                slot = c.racc_sb[:, ch * 1024 + tc * 512: ch * 1024 + tc * 512 + 512]
                p.op("vector", lambda e, slot=slot: e.tensor_tensor(
                    slot, c.abrow[:, 0:512], slot, ALU.mult),
                    waits=[("ex", eB)] if ch == 0 else ())
                fin = p.op("vector", lambda e, slot=slot: e.tensor_tensor(
                    slot, c.abrow[:, 512:1024], slot, ALU.add), inc=("ve", 1))
            p.op("sync", lambda e, pi=pi, tc=tc: e.dma_start(
                out=c.yT.ap().rearrange("(ch p) t -> p ch t", p=128)[
                    :, :, pi * TT + tc * 512: pi * TT + tc * 512 + 512],
                in_=c.racc_sb[:, :].rearrange("p (ch t) -> p ch t", ch=16)[
                    :, :, tc * 512: tc * 512 + 512]),
                waits=[("ve", fin)], inc=("st", 16))
        p.barrier()



def finish(nc, p):
    with nc.Block() as block:
        @block.tensor
        def _(t):
            p.replay("tensor", t)

        @block.vector
        def _(v):
            p.replay("vector", v)

        @block.scalar
        def _(s):
            p.replay("scalar", s)

        @block.gpsimd
        def _(g):
            p.replay("gpsimd", g)

        @block.sync
        def _(sy):
            p.replay("sync", sy)


def build_nc(debug_phase=0):
    nc = bass.Bass("TRN2", target_bir_lowering=False)
    c = declare(nc, debug_phase)
    p = Prog(nc)
    build_consts(p, c)
    build_fold(p, c)
    build_ffn(p, c)
    p.wait("sync", [("st", p.val("st"))])
    finish(nc, p)
    return nc, c


def make_in_maps(inputs):
    x = np.asarray(inputs["L0_attn_x"], dtype=np.float32)
    C = np.ascontiguousarray
    xT = [C(x[b].T) for b in range(B)]
    bv = np.asarray(inputs["L0_attn_bv"], np.float32)
    shared = {
        "wvT": C(np.asarray(inputs["L0_attn_Wv"], np.float32).T),
        "bv": C(bv.reshape(D // 128, 128).T),
        "woT": C(np.asarray(inputs["L0_attn_Wo"], np.float32).T),
        "bo": C(np.asarray(inputs["L0_attn_bo"], np.float32).reshape(D // 128, 128).T),
        "sw1T": C(np.asarray(inputs["L0_ffn_sw1"], np.float32).T),
        "sw2T": C(np.asarray(inputs["L0_ffn_sw2"], np.float32).T),
        "sw3T": C(np.asarray(inputs["L0_ffn_sw3"], np.float32).T),
        "w12T": C(np.asarray(inputs["L0_ffn_w12"], np.float32).transpose(0, 2, 1)),
        "w3T": C(np.asarray(inputs["L0_ffn_w3"], np.float32).transpose(0, 2, 1)),
        "rwT": C(np.asarray(inputs["L0_ffn_router_w"], np.float32).T),
        "ebias": C(np.tile(np.asarray(inputs["L0_ffn_expert_bias"], np.float32)[None, :],
                           (128, 1))),
        "ones": np.ones((128, 128), np.float32),
        "onesb": np.ones((128, 128), ml_dtypes.bfloat16),
        "identf": np.eye(128, dtype=np.float32),
        "idb": np.eye(128, dtype=np.float32).astype(ml_dtypes.bfloat16),
    }
    in_maps = []
    for cid in range(8):
        b, r = cid // 4, cid % 4
        im = dict(shared)
        im["xTq"] = C(xT[b][:, TQ * r: TQ * (r + 1)])
        in_maps.append(im)
    return in_maps


def kernel(**inputs):
    in_maps = make_in_maps(inputs)
    nc, c = build_nc(DEBUG_PHASE)
    res = run_bass_kernel_spmd(nc, in_maps, core_ids=list(range(8)))
    if DEBUG_PHASE:
        return res
    out = np.empty((B, S, D), np.float32)
    for cid in range(8):
        b, r = cid // 4, cid % 4
        out[b, TQ * r: TQ * (r + 1), :] = res.results[cid]["yT"].T
    return out
